# version 03
# baseline (speedup 1.0000x reference)
import contextlib

import numpy as np
import concourse.bass as bass
import concourse.mybir as mybir
from concourse.bass_utils import run_bass_kernel_spmd

F32 = mybir.dt.float32
BF16 = mybir.dt.bfloat16
I32 = mybir.dt.int32
AF = mybir.ActivationFunctionType
ALU = mybir.AluOpType
AX = mybir.AxisListType

NCORE = 8
B, T, D, S, DI = 2, 8192, 1024, 16, 2816
TPC = 2048
QMAX = 127.0
LN_EPS = 1e-5
DT = D // 128
IT = DI // 128


class Ctx:
    def __init__(self, nc, stack):
        self.nc = nc
        self.stack = stack
        self.sems = {}
        self.cnt = {}

    def sem(self, name):
        if name not in self.sems:
            self.sems[name] = self.stack.enter_context(self.nc.semaphore(name))
            self.cnt[name] = 0
        return self.sems[name]

    def inc(self, inst, name, k=1):
        self.sem(name)
        self.cnt[name] += k
        inst.then_inc(self.sems[name], k)
        return inst

    def dma(self, inst, name):
        return self.inc(inst, name, 16)

    def wait(self, eng, name, val=None):
        self.sem(name)
        eng.wait_ge(self.sems[name], self.cnt[name] if val is None else val)

    def vsync(self, eng):
        self.sem("vself")
        self.cnt["vself"] += 1
        eng.engine_nop().then_inc(self.sems["vself"], 1)
        eng.wait_ge(self.sems["vself"], self.cnt["vself"])

    def sb(self, name, shape, dtype=F32):
        return self.stack.enter_context(self.nc.sbuf_tensor(name, shape, dtype))

    def ps(self, name, shape, dtype=F32):
        return self.stack.enter_context(self.nc.psum_tensor(name, shape, dtype))


def emit_ident(cx, blk, ident_f32, ident_bf16):
    @blk.gpsimd
    def _(g):
        cx.inc(g.iota(ident_f32[:], pattern=[[1, 128]], base=0, channel_multiplier=-1,
                      allow_small_or_imprecise_dtypes=True), "idg0")
        cx.wait(g, "idg0")
        cx.inc(g.tensor_single_scalar(ident_f32[:], ident_f32[:], 0.0, ALU.is_equal),
               "idg")

    @blk.vector
    def _(v):
        cx.wait(v, "idg")
        cx.inc(v.tensor_copy(ident_bf16[:], ident_f32[:]), "idb")


def emit_round(v, src, i32_ap, d_ap, c1_ap, out_ap):
    v.tensor_copy(i32_ap, src)
    v.tensor_tensor(d_ap, src, i32_ap, ALU.subtract)
    v.scalar_tensor_tensor(c1_ap, d_ap, -0.5, i32_ap, ALU.is_lt, ALU.subtract)
    v.scalar_tensor_tensor(out_ap, d_ap, 0.5, c1_ap, ALU.is_gt, ALU.subtract)


def emit_wsplit_transpose(cx, nc, w_stage, wh_nat, wl_nat, dst_h, dst_l, identb,
                          ptb, ntile):
    ncol = ntile * 128
    for j in range(ntile):
        with nc.Block() as blk:
            @blk.vector
            def _(v):
                if j == 0:
                    cx.wait(v, "ldw")
                    cx.wait(v, "idb")
                src = w_stage[:, j * ncol:(j + 1) * ncol]
                v.tensor_copy(wh_nat[:], src)
                cx.inc(v.tensor_tensor(wl_nat[:], src, wh_nat[:], ALU.subtract), "vsp")

            @blk.tensor
            def _(t):
                cx.wait(t, "vsp")
                base_wev = cx.cnt["wev"] if "wev" in cx.cnt else 0
                cx.sem("wev")
                base_wev = cx.cnt["wev"]
                for k in range(ntile):
                    if k >= 1:
                        cx.wait(t, "wev", base_wev + k)
                    t.transpose(ptb[0][:, 0:128], wh_nat[:, k * 128:(k + 1) * 128],
                                identb[:])
                    cx.inc(t.transpose(ptb[1][:, 0:128],
                                       wl_nat[:, k * 128:(k + 1) * 128], identb[:]),
                           "wtr")

            @blk.vector
            def _(v):
                for k in range(ntile):
                    cx.wait(v, "wtr", cx.cnt["wtr"] - (ntile - 1 - k))
                    v.tensor_copy(dst_h[:, k * ncol + j * 128: k * ncol + (j + 1) * 128],
                                  ptb[0][:, 0:128])
                    cx.inc(v.tensor_copy(
                        dst_l[:, k * ncol + j * 128: k * ncol + (j + 1) * 128],
                        ptb[1][:, 0:128]), "wev")

            @blk.tensor
            def _(t):
                cx.wait(t, "wev")


def emit_ln_chunk(cx, nc, c, TT, xsrc, lnout, sqs, stat, amax=None, amaxt=None,
                  tag="", ready=None):
    sq3, st3, rv3, rs3, lnd = "sq3" + tag, "st3" + tag, "rv3" + tag, "rs3" + tag, \
        "lnd" + tag
    with nc.Block() as blk:
        @blk.scalar
        def _(s):
            for a in range(TT):
                g = c * TT + a
                if g >= 1:
                    cx.wait(s, st3, g)
                if ready is not None:
                    ready(s, a)
                cx.inc(s.activation(sqs[:], xsrc[:, a * D:(a + 1) * D], AF.Square),
                       sq3)
                cx.wait(s, rv3, g + 1)
                cx.inc(s.activation(stat[:, 3:4], stat[:, 2:3], AF.Sqrt), rs3)

        @blk.vector
        def _(v):
            for a in range(TT):
                g = c * TT + a
                cx.wait(v, sq3, g + 1)
                xrow = xsrc[:, a * D:(a + 1) * D]
                m = stat[:, 0:1]
                r = stat[:, 1:2]
                t1 = stat[:, 2:3]
                t2 = stat[:, 3:4]
                v.reduce_sum(m, xrow, axis=AX.X)
                v.reduce_sum(r, sqs[:], axis=AX.X)
                cx.vsync(v)
                v.tensor_scalar_mul(m, m, 1.0 / D)
                cx.inc(v.tensor_scalar_mul(r, r, 1.0 / D), st3)
                cx.vsync(v)
                v.tensor_mul(t1, m, m)
                cx.vsync(v)
                v.tensor_tensor(r, r, t1, ALU.subtract)
                cx.vsync(v)
                v.tensor_scalar_add(t1, r, LN_EPS)
                cx.vsync(v)
                cx.inc(v.tensor_copy(t2, t1), rv3)
                cx.wait(v, rs3, g + 1)
                v.reciprocal(r, t2)
                cx.vsync(v)
                v.tensor_mul(t2, r, r)
                cx.vsync(v)
                v.tensor_mul(t2, t1, t2)
                cx.vsync(v)
                v.tensor_scalar(t2, t2, -0.5, 1.5, ALU.mult, ALU.add)
                cx.vsync(v)
                v.tensor_mul(r, r, t2)
                cx.vsync(v)
                lrow = lnout[:, a * D:(a + 1) * D]
                v.tensor_scalar(lrow, xrow, m, r, ALU.subtract, ALU.mult)
                if amax is not None:
                    v.reduce_max(amaxt[:], lrow, axis=AX.X, apply_absolute_value=True)
                    cx.vsync(v)
                    v.tensor_max(amax[:], amax[:], amaxt[:])
                cx.inc(v.tensor_copy(stat[:, 0:1], stat[:, 0:1]), lnd)



def emit_localmax_finish(cx, nc, amax, ident, ptr, fin, fin2, out_dram):
    with nc.Block() as blk:
        @blk.tensor
        def _(t):
            cx.wait(t, "allmax")
            cx.inc(t.transpose(ptr[0][0:1, 0:128], amax[:, 0:1], ident[:]), "ftr")

        @blk.vector
        def _(v):
            cx.wait(v, "ftr")
            v.tensor_copy(fin[0:1, :], ptr[0][0:1, 0:128])
            cx.vsync(v)
            cx.inc(v.reduce_max(fin2[0:1, 0:1], fin[0:1, :], axis=AX.X), "fmx")

        @blk.sync
        def _(sync):
            cx.wait(sync, "fmx")
            cx.dma(sync.dma_start(out_dram.ap()[0:1, 0:1], fin2[0:1, 0:1]), "stf")
            cx.wait(sync, "stf")


def build_l1(nch=None, stop="full", dbg=False):
    nc = bass.Bass("TRN2", target_bir_lowering=False, debug=False,
                   detect_race_conditions=False)
    x_in = nc.dram_tensor("x", [TPC, D], F32, kind="ExternalInput")
    wd_in = nc.dram_tensor("wd", [D, D], F32, kind="ExternalInput")
    bd_in = nc.dram_tensor("bd", [D], F32, kind="ExternalInput")
    delta_out = nc.dram_tensor("delta", [D, TPC], F32, kind="ExternalOutput")
    v_out = nc.dram_tensor("vv", [D, TPC], F32, kind="ExternalOutput")
    if dbg:
        dbg_xlnT = nc.dram_tensor("dbg_xlnT", [128, DT * 512], F32,
                                  kind="ExternalOutput")
        dbg_wdth = nc.dram_tensor("dbg_wdth", [128, D], F32,
                                  kind="ExternalOutput")
        dbg_xh = nc.dram_tensor("dbg_xh", [128, DT * 512], F32,
                                kind="ExternalOutput")

    stack = contextlib.ExitStack()
    cx = Ctx(nc, stack)
    TC = 512
    NCH = TPC // TC if nch is None else nch
    TT = TC // 128

    ident = cx.sb("ident", [128, 128], F32)
    identb = cx.sb("identb", [128, 128], BF16)
    wstage = cx.sb("wstage", [128, DT * D], F32)
    wh_nat = cx.sb("wh_nat", [128, D], BF16)
    wl_nat = cx.sb("wl_nat", [128, D], BF16)
    wdt_h = cx.sb("wdt_h", [128, DT * D], BF16)
    wdt_l = cx.sb("wdt_l", [128, DT * D], BF16)
    bd_sb = cx.sb("bd_sb", [128, DT], F32)
    bd_row = cx.sb("bd_row", [1, D], F32)
    xstage = cx.sb("xstage", [128, TT * D], F32)
    xln_tm = cx.sb("xln_tm", [128, TT * D], F32)
    sqs = cx.sb("sqs", [128, D], F32)
    stat = cx.sb("stat", [128, 4], F32)
    xlnT = cx.sb("xlnT", [128, DT * TC], F32)
    xh = cx.sb("xh", [128, DT * TC], BF16)
    xl = cx.sb("xl", [128, DT * TC], BF16)
    dstage = cx.sb("dstage", [128, TC], F32)
    vstage = cx.sb("vstage", [128, TC], F32)
    sp0 = cx.sb("sp0", [128, TC], F32)
    sp1 = cx.sb("sp1", [128, TC], F32)
    mx0 = cx.sb("mx0", [128, TC], F32)

    ptr = [cx.ps(f"ptr{i}", [128, 128], F32) for i in range(2)]
    ptb = [cx.ps(f"ptb{i}", [128, 128], BF16) for i in range(2)]
    pmm = [cx.ps(f"pmm{i}", [128, TC], F32) for i in range(2)]

    with nc.Block() as blk:
        emit_ident(cx, blk, ident, identb)

        @blk.sync
        def _(sync):
            for j in range(DT):
                cx.dma(sync.dma_start(wstage[:, j * D:(j + 1) * D],
                                      wd_in.ap()[j * 128:(j + 1) * 128, :]), "ldw")
            cx.dma(sync.dma_start(
                bd_row[0:1, :], bd_in.ap().rearrange("(a d) -> a d", a=1)), "ldw")

    emit_wsplit_transpose(cx, nc, wstage, wh_nat, wl_nat, wdt_h, wdt_l, identb,
                          ptb, DT)

    with nc.Block() as blk:
        @blk.tensor
        def _(t):
            cx.wait(t, "ldw")
            cx.wait(t, "idg")
            for j in range(DT):
                if j >= 2:
                    cx.wait(t, "bdev", j - 1)
                cx.inc(t.transpose(ptr[j % 2][:, 0:1],
                                   bd_row[0:1, j * 128:(j + 1) * 128],
                                   ident[0:1, 0:1]), "bdtr")

        @blk.vector
        def _(v):
            for j in range(DT):
                cx.wait(v, "bdtr", j + 1)
                cx.inc(v.tensor_copy(bd_sb[:, j:j + 1], ptr[j % 2][:, 0:1]), "bdev")

        @blk.tensor
        def _(t):
            cx.wait(t, "bdev")

    if stop == "w":
        stack.close()
        return nc
    for c in range(NCH):
        with nc.Block() as blk:
            @blk.sync
            def _(sync):
                if c > 0:
                    cx.wait(sync, "xfree", c)
                cx.dma(sync.dma_start(
                    xstage.ap().rearrange("p (a d) -> p a d", d=D),
                    x_in.ap()[c * TC:(c + 1) * TC, :].rearrange(
                        "(a p) d -> p a d", p=128)), "ldx")

        ldx_need = cx.cnt["ldx"]
        emit_ln_chunk(cx, nc, c, TT, xstage, xln_tm, sqs, stat, tag="l1",
                      ready=lambda s, a, n=ldx_need: cx.wait(s, "ldx", n))
        if stop == "ln":
            continue

        with nc.Block() as blk:
            @blk.tensor
            def _(t):
                cx.sem("xev")
                cx.wait(t, "lndl1")
                for k in range(DT):
                    for a in range(TT):
                        g = k * TT + a
                        pt = ptr[g % 2]
                        if cx.cnt["xev"] >= 2 or g >= 2:
                            cx.wait(t, "xev", cx.cnt["xev"] + g - 1)
                        cx.inc(t.transpose(
                            pt[:, 0:128],
                            xln_tm[:, a * D + k * 128: a * D + (k + 1) * 128],
                            ident[:]), "xtr")

            @blk.vector
            def _(v):
                base = cx.cnt["xtr"] - DT * TT
                for k in range(DT):
                    for a in range(TT):
                        g = k * TT + a
                        cx.wait(v, "xtr", base + g + 1)
                        cx.inc(v.tensor_copy(
                            xlnT[:, k * TC + a * 128: k * TC + (a + 1) * 128],
                            ptr[g % 2][:, 0:128]), "xev")
                for k in range(DT):
                    v.tensor_copy(xh[:, k * TC:(k + 1) * TC],
                                  xlnT[:, k * TC:(k + 1) * TC])
                    cx.inc(v.tensor_tensor(xl[:, k * TC:(k + 1) * TC],
                                           xlnT[:, k * TC:(k + 1) * TC],
                                           xh[:, k * TC:(k + 1) * TC], ALU.subtract),
                           "xsp")
                cx.inc(v.tensor_copy(stat[:, 0:1], stat[:, 0:1]), "xfree")

        if stop == "tr":
            continue
        if dbg and c == 0:
            with nc.Block() as blk:
                @blk.vector
                def _(v):
                    cx.wait(v, "xsp")
                    v.tensor_copy(xln_tm[:, 0:DT * 512], xlnT[:, :])
                    v.tensor_copy(sqs[:, 0:D], wdt_h[:, 0:D])
                    cx.inc(v.tensor_copy(xstage[:, 0:DT * 512], xh[:, :]), "dbgv")

                @blk.sync
                def _(sync):
                    cx.wait(sync, "dbgv")
                    cx.dma(sync.dma_start(dbg_xlnT.ap(), xln_tm[:, 0:DT * 512]), "dbg")
                    cx.dma(sync.dma_start(dbg_wdth.ap(), sqs[:, 0:D]), "dbg")
                    cx.dma(sync.dma_start(dbg_xh.ap(), xstage[:, 0:DT * 512]), "dbg")
                    cx.wait(sync, "dbg")
        for et in range(DT):
            g = c * DT + et
            with nc.Block() as blk:
                pm = pmm[et % 2]

                @blk.tensor
                def _(t):
                    if et == 0:
                        cx.wait(t, "xsp")
                    if g >= 2:
                        cx.wait(t, "spev", g - 1)
                    first = True
                    for k in range(DT):
                        lh = wdt_h[:, k * D + et * 128: k * D + (et + 1) * 128]
                        ll = wdt_l[:, k * D + et * 128: k * D + (et + 1) * 128]
                        rh = xh[:, k * TC:(k + 1) * TC]
                        rl = xl[:, k * TC:(k + 1) * TC]
                        for lhs, rhs in ((lh, rh), (lh, rl), (ll, rh)):
                            last = (k == DT - 1) and (lhs is ll)
                            mi = t.matmul(pm[:, :], lhs, rhs, start=first, stop=last)
                            if last:
                                cx.inc(mi, "mm")
                            first = False

                if stop == "mm":
                    @blk.vector
                    def _(v):
                        cx.wait(v, "mm")
                        cx.inc(v.tensor_copy(dstage[:], pm[:, :]), "spev")
                    continue

                @blk.scalar
                def _(s):
                    cx.wait(s, "mm")
                    bcol = bd_sb[:, et:et + 1]
                    s.activation(sp0[:], pm[:, :], AF.Abs, bias=bcol)
                    s.activation(sp1[:], sp0[:], AF.Exp, scale=-1.0)
                    cx.inc(s.activation(sp0[:], sp1[:], AF.Ln, bias=1.0), "spl")

                @blk.vector
                def _(v):
                    cx.wait(v, "mm")
                    if g >= 1 and stop != "ev":
                        cx.wait(v, "sto", 32 * g)
                    v.tensor_scalar(mx0[:], pm[:, :], bd_sb[:, et:et + 1], 0.0,
                                    ALU.add, ALU.max)
                    cx.wait(v, "spl")
                    v.tensor_add(dstage[:], mx0[:], sp0[:])
                    cx.inc(v.tensor_mul(vstage[:], dstage[:],
                                        xlnT[:, et * TC:(et + 1) * TC]), "spev")

                if stop == "ev":
                    continue

                @blk.sync
                def _(sync):
                    cx.wait(sync, "spev")
                    cx.dma(sync.dma_start(
                        delta_out.ap()[et * 128:(et + 1) * 128, c * TC:(c + 1) * TC],
                        dstage[:]), "sto")
                    cx.dma(sync.dma_start(
                        v_out.ap()[et * 128:(et + 1) * 128, c * TC:(c + 1) * TC],
                        vstage[:]), "sto")

    with nc.Block() as blk:
        @blk.sync
        def _(sync):
            cx.wait(sync, "sto")
    stack.close()
    return nc


def build_l2():
    nc = bass.Bass("TRN2", target_bir_lowering=False, debug=False,
                   detect_race_conditions=False)
    TT2 = B * T
    TC2 = 1024
    NCH = TT2 // TC2
    BB = T // TC2

    d_in = nc.dram_tensor("delta_sl", [128, TT2], F32, kind="ExternalInput")
    v_in = nc.dram_tensor("v_sl", [128, TT2], F32, kind="ExternalInput")
    alog_in = nc.dram_tensor("alog_sl", [128, S], F32, kind="ExternalInput")
    bpar_in = nc.dram_tensor("bpar_sl", [128, S], F32, kind="ExternalInput")
    y_out = nc.dram_tensor("y_sl", [128, TT2], F32, kind="ExternalOutput")

    stack = contextlib.ExitStack()
    cx = Ctx(nc, stack)

    dbuf = [cx.sb(f"dbuf{i}", [128, TC2], F32) for i in range(2)]
    vbuf = [cx.sb(f"vbuf{i}", [128, TC2], F32) for i in range(2)]
    abar = [cx.sb(f"abar{i}", [128, TC2], F32) for i in range(2)]
    sig = cx.sb("sig", [128, S * TC2], F32)
    ybuf = [cx.sb(f"ybuf{i}", [128, TC2], F32) for i in range(2)]
    carry = cx.sb("carry", [128, S], F32)
    apos = cx.sb("apos", [128, S], F32)
    nega = cx.sb("nega", [128, S], F32)
    bpar = cx.sb("bpar", [128, S], F32)
    t0 = cx.sb("t0", [128, S], F32)
    t1 = cx.sb("t1", [128, S], F32)

    with nc.Block() as blk:
        @blk.sync
        def _(sync):
            cx.dma(sync.dma_start(t0[:], alog_in.ap()), "ld")
            cx.dma(sync.dma_start(bpar[:], bpar_in.ap()), "ld")
            cx.dma(sync.dma_start(dbuf[0][:], d_in.ap()[:, 0:TC2]), "ldd")
            cx.dma(sync.dma_start(vbuf[0][:], v_in.ap()[:, 0:TC2]), "ldd")

        @blk.scalar
        def _(s):
            cx.wait(s, "ld")
            s.activation(t1[:], t0[:], AF.Abs)
            s.activation(t1[:], t1[:], AF.Exp, scale=-1.0)
            cx.inc(s.activation(t1[:], t1[:], AF.Ln, bias=1.0), "sps")

        @blk.vector
        def _(v):
            cx.wait(v, "sps")
            v.tensor_scalar_max(apos[:], t0[:], 0.0)
            cx.vsync(v)
            v.tensor_add(apos[:], apos[:], t1[:])
            cx.vsync(v)
            v.tensor_scalar_add(apos[:], apos[:], 1e-4)
            cx.vsync(v)
            cx.inc(v.tensor_scalar_mul(nega[:], apos[:], -1.0), "aps")
            cx.vsync(v)

    for c in range(NCH):
        pb = c % 2
        with nc.Block() as blk:
            @blk.sync
            def _(sync):
                if c + 1 < NCH:
                    if c >= 1:
                        cx.wait(sync, "scand", c)
                    cx.dma(sync.dma_start(dbuf[1 - pb][:],
                                          d_in.ap()[:, (c + 1) * TC2:(c + 2) * TC2]),
                           "ldd")
                    cx.dma(sync.dma_start(vbuf[1 - pb][:],
                                          v_in.ap()[:, (c + 1) * TC2:(c + 2) * TC2]),
                           "ldd")

            @blk.scalar
            def _(s):
                if c == 0:
                    cx.wait(s, "aps")
                s.wait_ge(cx.sems["ldd"], (c + 1) * 32)
                for si in range(S):
                    if c * S + si >= 2:
                        cx.wait(s, "scs", c * S + si - 1)
                    cx.inc(s.activation(abar[si % 2][:], dbuf[pb][:], AF.Exp,
                                        scale=nega[:, si:si + 1]), "se")

            @blk.vector
            def _(v):
                for si in range(S):
                    cx.wait(v, "se", c * S + si + 1)
                    if c == 0 or c == BB:
                        init = 0.0
                    else:
                        init = carry[:, si:si + 1]
                    cx.inc(v.tensor_tensor_scan(
                        sig[:, si * TC2:(si + 1) * TC2], abar[si % 2][:], vbuf[pb][:],
                        init, ALU.mult, ALU.add), "scs")
                cx.vsync(v)
                v.tensor_copy(
                    carry[:, :],
                    sig.ap().rearrange("p (s t) -> p s t", s=S)[:, :, TC2 - 1:TC2])
                cx.vsync(v)
                yb = ybuf[pb]
                if c >= 2:
                    cx.wait(v, "sty", 16 * (c - 1))
                v.tensor_scalar_mul(yb[:], sig[:, 0:TC2], bpar[:, 0:1])
                for si in range(1, S):
                    v.scalar_tensor_tensor(yb[:], sig[:, si * TC2:(si + 1) * TC2],
                                           bpar[:, si:si + 1], yb[:],
                                           ALU.mult, ALU.add)
                cx.inc(v.tensor_copy(t0[:, 0:1], t0[:, 0:1]), "scand")
                cx.inc(v.tensor_copy(t1[:, 0:1], t1[:, 0:1]), "ydone")

            @blk.sync
            def _(sync):
                cx.wait(sync, "ydone")
                cx.dma(sync.dma_start(y_out.ap()[:, c * TC2:(c + 1) * TC2],
                                      ybuf[pb][:]), "sty")

    with nc.Block() as blk:
        @blk.sync
        def _(sync):
            cx.wait(sync, "sty")
    stack.close()
    return nc


def build_l3():
    nc = bass.Bass("TRN2", target_bir_lowering=False, debug=False,
                   detect_race_conditions=False)
    x_in = nc.dram_tensor("x", [TPC, D], F32, kind="ExternalInput")
    y_in = nc.dram_tensor("y", [D, TPC], F32, kind="ExternalInput")
    wo_in = nc.dram_tensor("wo", [D, D], F32, kind="ExternalInput")
    bo_in = nc.dram_tensor("bo", [D], F32, kind="ExternalInput")
    x2_out = nc.dram_tensor("x2", [TPC, D], F32, kind="ExternalOutput")
    ln2_out = nc.dram_tensor("ln2", [TPC, D], F32, kind="ExternalOutput")
    lmax_out = nc.dram_tensor("lmax", [1, 1], F32, kind="ExternalOutput")

    stack = contextlib.ExitStack()
    cx = Ctx(nc, stack)
    TC = 512
    NCH = TPC // TC
    TT = TC // 128
    EB = 2

    ident = cx.sb("ident", [128, 128], F32)
    identb = cx.sb("identb", [128, 128], BF16)
    wstage = cx.sb("wstage", [128, DT * D], F32)
    wh_nat = cx.sb("wh_nat", [128, D], BF16)
    wl_nat = cx.sb("wl_nat", [128, D], BF16)
    wot_h = cx.sb("wot_h", [128, DT * D], BF16)
    wot_l = cx.sb("wot_l", [128, DT * D], BF16)
    bo_bc = cx.sb("bo_bc", [128, D], F32)
    onesr = cx.sb("onesr", [1, 128], F32)
    bo_row = cx.sb("bo_row", [1, D], F32)
    yfm = cx.sb("yfm", [128, DT * TC], F32)
    yh = cx.sb("yh", [128, DT * TC], BF16)
    yl = cx.sb("yl", [128, DT * TC], BF16)
    xstage = cx.sb("xstage", [128, TT * D], F32)
    x2s = cx.sb("x2s", [128, TT * D], F32)
    ln2s = cx.sb("ln2s", [128, TT * D], F32)
    stat = cx.sb("stat", [128, 4], F32)
    sqs = cx.sb("sqs", [128, D], F32)
    amax = cx.sb("amax", [128, 1], F32)
    amaxt = cx.sb("amaxt", [128, 1], F32)
    fin = cx.sb("fin", [1, 128], F32)
    fin2 = cx.sb("fin2", [1, 1], F32)

    ptr = [cx.ps(f"ptr{i}", [128, 128], F32) for i in range(2)]
    ptb = [cx.ps(f"ptb{i}", [128, 128], BF16) for i in range(2)]
    pmm = [cx.ps(f"pmm{i}", [128, 512], F32) for i in range(2)]

    with nc.Block() as blk:
        emit_ident(cx, blk, ident, identb)

        @blk.sync
        def _(sync):
            for j in range(DT):
                cx.dma(sync.dma_start(wstage[:, j * D:(j + 1) * D],
                                      wo_in.ap()[j * 128:(j + 1) * 128, :]), "ldw")
            cx.dma(sync.dma_start(bo_row[0:1, :],
                                  bo_in.ap().rearrange("(a d) -> a d", a=1)), "ldw")

        @blk.vector
        def _(v):
            cx.wait(v, "ldw")
            v.memset(onesr[0:1, :], 1.0)
            cx.inc(v.memset(amax[:], 0.0), "ini")

        @blk.tensor
        def _(t):
            cx.wait(t, "ini")
            for eb in range(EB):
                cx.inc(t.matmul(pmm[eb][:, 0:512], onesr[0:1, :],
                                bo_row[0:1, eb * 512:(eb + 1) * 512],
                                start=True, stop=True), "bo1")

        @blk.vector
        def _(v):
            cx.wait(v, "bo1")
            for eb in range(EB):
                cx.inc(v.tensor_copy(bo_bc[:, eb * 512:(eb + 1) * 512],
                                     pmm[eb][:, 0:512]), "bo2")

        @blk.tensor
        def _(t):
            cx.wait(t, "bo2")

    emit_wsplit_transpose(cx, nc, wstage, wh_nat, wl_nat, wot_h, wot_l, identb,
                          ptb, DT)

    for c in range(NCH):
        with nc.Block() as blk:
            @blk.sync
            def _(sync):
                if c > 0:
                    cx.wait(sync, "chend", c)
                cx.dma(sync.dma_start(
                    xstage.ap().rearrange("p (a d) -> p a d", d=D),
                    x_in.ap()[c * TC:(c + 1) * TC, :].rearrange(
                        "(a p) d -> p a d", p=128)), "ldx")
                cx.dma(sync.dma_start(
                    yfm.ap().rearrange("p (k t) -> p k t", t=TC),
                    y_in.ap()[:, c * TC:(c + 1) * TC].rearrange(
                        "(k p) t -> p k t", p=128)), "ldx")

            @blk.vector
            def _(v):
                cx.wait(v, "ldx")
                for k in range(DT):
                    v.tensor_copy(yh[:, k * TC:(k + 1) * TC], yfm[:, k * TC:(k + 1) * TC])
                    cx.inc(v.tensor_tensor(yl[:, k * TC:(k + 1) * TC],
                                           yfm[:, k * TC:(k + 1) * TC],
                                           yh[:, k * TC:(k + 1) * TC], ALU.subtract),
                           "ysp")

        for a in range(TT):
            gt = c * TT + a
            with nc.Block() as blk:
                @blk.tensor
                def _(t):
                    if a == 0:
                        cx.wait(t, "ysp")
                    for eb in range(EB):
                        gg = 2 * gt + eb
                        if gg >= 2:
                            cx.wait(t, "x2ev", gg - 1)
                        first = True
                        for k in range(DT):
                            lh = yh[:, k * TC + a * 128: k * TC + (a + 1) * 128]
                            ll = yl[:, k * TC + a * 128: k * TC + (a + 1) * 128]
                            rh = wot_h[:, k * D + eb * 512: k * D + (eb + 1) * 512]
                            rl = wot_l[:, k * D + eb * 512: k * D + (eb + 1) * 512]
                            for lhs, rhs in ((lh, rh), (lh, rl), (ll, rh)):
                                last = (k == DT - 1) and (lhs is ll)
                                mi = t.matmul(pmm[eb][:, 0:512], lhs, rhs,
                                              start=first, stop=last)
                                if last:
                                    cx.inc(mi, "mmo")
                                first = False

                @blk.vector
                def _(v):
                    xrow = xstage[:, a * D:(a + 1) * D]
                    x2row = x2s[:, a * D:(a + 1) * D]
                    if gt == 0:
                        pass
                    for eb in range(EB):
                        cx.wait(v, "mmo", cx.cnt["mmo"] - (EB - 1 - eb))
                        sl = slice(eb * 512, (eb + 1) * 512)
                        v.tensor_add(x2row[:, sl], pmm[eb][:, 0:512], xrow[:, sl])
                        cx.inc(v.tensor_add(x2row[:, sl], x2row[:, sl], bo_bc[:, sl]),
                               "x2ev")

        emit_ln_chunk(cx, nc, c, TT, x2s, ln2s, sqs, stat, amax=amax, amaxt=amaxt,
                      tag="l3",
                      ready=lambda s, a, c0=c: cx.wait(s, "x2ev",
                                                       2 * (c0 * TT + a + 1)))

        with nc.Block() as blk:
            @blk.sync
            def _(sync):
                cx.wait(sync, "lndl3")
                cx.dma(sync.dma_start(
                    x2_out.ap()[c * TC:(c + 1) * TC, :].rearrange(
                        "(a p) d -> p a d", p=128),
                    x2s.ap().rearrange("p (a d) -> p a d", d=D)), "sto")
                cx.dma(sync.dma_start(
                    ln2_out.ap()[c * TC:(c + 1) * TC, :].rearrange(
                        "(a p) d -> p a d", p=128),
                    ln2s.ap().rearrange("p (a d) -> p a d", d=D)), "sto")
                cx.wait(sync, "sto")
                cx.inc(sync.nop(), "chend")
    cx.cnt["allmax"] = cx.cnt["chend"]
    cx.sems["allmax"] = cx.sems["chend"]

    emit_localmax_finish(cx, nc, amax, ident, ptr, fin, fin2, lmax_out)
    stack.close()
    return nc


def build_l4():
    nc = bass.Bass("TRN2", target_bir_lowering=False, debug=False,
                   detect_race_conditions=False)
    ln2_in = nc.dram_tensor("ln2", [TPC, D], F32, kind="ExternalInput")
    lmax_in = nc.dram_tensor("lmaxs", [1, NCORE], F32, kind="ExternalInput")
    clip_in = nc.dram_tensor("clip_x", [1, 1], F32, kind="ExternalInput")
    wg_in = nc.dram_tensor("wg", [DI, D], F32, kind="ExternalInput")
    wu_in = nc.dram_tensor("wu", [DI, D], F32, kind="ExternalInput")
    h_out = nc.dram_tensor("h", [DI, TPC], F32, kind="ExternalOutput")
    lmax2_out = nc.dram_tensor("lmax2", [1, 1], F32, kind="ExternalOutput")

    stack = contextlib.ExitStack()
    cx = Ctx(nc, stack)
    TC = 512
    NCH = TPC // TC
    TT = TC // 128

    ident = cx.sb("ident", [128, 128], F32)
    identb = cx.sb("identb", [128, 128], BF16)
    wstrip = cx.sb("wstrip", [128, D], F32)
    wq_nat = cx.sb("wq_nat", [128, D], BF16)
    wgt = cx.sb("wgt", [128, DT * DI], BF16)
    wut = cx.sb("wut", [128, DT * DI], BF16)
    sg = cx.sb("sg", [128, IT], F32)
    su = cx.sb("su", [128, IT], F32)
    rowm = cx.sb("rowm", [128, 1], F32)
    rs_w = cx.sb("rs_w", [128, 1], F32)
    i32b = cx.sb("i32b", [128, D], I32)
    db = cx.sb("db", [128, D], F32)
    c1b = cx.sb("c1b", [128, D], F32)
    gl = cx.sb("gl", [1, NCORE], F32)
    gmax = cx.sb("gmax", [1, 1], F32)
    clip_sb = cx.sb("clip_sb", [1, 1], F32)
    sx = cx.sb("sx", [1, 1], F32)
    rsx = cx.sb("rsx", [1, 1], F32)
    onesr = cx.sb("onesr", [1, 128], F32)
    rsx_bc = cx.sb("rsx_bc", [128, 1], F32)
    sx_bc = cx.sb("sx_bc", [128, 1], F32)
    lstage = cx.sb("lstage", [128, TT * D], F32)
    xq_tm = cx.sb("xq_tm", [128, TT * D], BF16)
    tq = cx.sb("tq", [128, D], F32)
    xqT = cx.sb("xqT", [128, DT * TC], BF16)
    silu = cx.sb("silu", [128, TC], F32)
    hst = cx.sb("hst", [128, TC], F32)
    amax = cx.sb("amax", [128, 1], F32)
    amaxt = cx.sb("amaxt", [128, 1], F32)
    fin = cx.sb("fin", [1, 128], F32)
    fin2 = cx.sb("fin2", [1, 1], F32)

    ptr = [cx.ps(f"ptr{i}", [128, 128], F32) for i in range(2)]
    ptb = [cx.ps(f"ptb{i}", [128, 128], BF16) for i in range(2)]
    pmm = [cx.ps(f"pmm{i}", [128, 512], F32) for i in range(4)]

    with nc.Block() as blk:
        emit_ident(cx, blk, ident, identb)

        @blk.sync
        def _(sync):
            cx.dma(sync.dma_start(gl[0:1, :], lmax_in.ap()[0:1, :]), "ldc")
            cx.dma(sync.dma_start(clip_sb[0:1, :], clip_in.ap()[0:1, :]), "ldc")

        @blk.vector
        def _(v):
            cx.wait(v, "ldc")
            cx.inc(v.reduce_max(gmax[0:1, 0:1], gl[0:1, :], axis=AX.X), "gm")
            cx.vsync(v)
            v.memset(onesr[0:1, :], 1.0)
            v.memset(amax[:], 0.0)

        @blk.scalar
        def _(s):
            cx.wait(s, "gm")
            cx.inc(s.activation(sx[0:1, :], clip_sb[0:1, :], AF.Sigmoid), "sg1")

        @blk.vector
        def _(v):
            cx.wait(v, "sg1")
            v.tensor_mul(sx[0:1, :], sx[0:1, :], gmax[0:1, :])
            cx.vsync(v)
            v.tensor_scalar_mul(sx[0:1, :], sx[0:1, :], 1.0 / QMAX)
            cx.vsync(v)
            cx.inc(v.reciprocal(rsx[0:1, :], sx[0:1, :]), "rsx1")
            cx.vsync(v)

        @blk.tensor
        def _(t):
            cx.wait(t, "rsx1")
            t.matmul(ptr[0][:, 0:1], onesr[0:1, :], rsx[0:1, 0:1],
                     start=True, stop=True)
            cx.inc(t.matmul(ptr[1][:, 0:1], onesr[0:1, :], sx[0:1, 0:1],
                            start=True, stop=True), "rbx")

        @blk.vector
        def _(v):
            cx.wait(v, "rbx")
            v.tensor_copy(rsx_bc[:, 0:1], ptr[0][:, 0:1])
            cx.inc(v.tensor_copy(sx_bc[:, 0:1], ptr[1][:, 0:1]), "rbx2")

        @blk.tensor
        def _(t):
            cx.wait(t, "rbx2")

    for wi, (w_in, wqt, scol) in enumerate(((wg_in, wgt, sg), (wu_in, wut, su))):
        for it in range(IT):
            gw = wi * IT + it
            with nc.Block() as blk:
                @blk.sync
                def _(sync):
                    if gw > 0:
                        cx.wait(sync, "wt0", gw)
                    cx.dma(sync.dma_start(wstrip[:, :],
                                          w_in.ap()[it * 128:(it + 1) * 128, :]),
                           "ldw")

                @blk.vector
                def _(v):
                    cx.wait(v, "ldw")
                    v.reduce_max(rowm[:], wstrip[:, :], axis=AX.X,
                                 apply_absolute_value=True)
                    cx.vsync(v)
                    v.reciprocal(rs_w[:], rowm[:])
                    cx.vsync(v)
                    cx.inc(v.tensor_scalar_mul(rs_w[:], rs_w[:], QMAX), "wrs")
                    cx.inc(v.tensor_scalar_mul(scol[:, it:it + 1], rowm[:],
                                               1.0 / QMAX), "wsc")

                @blk.scalar
                def _(s):
                    cx.wait(s, "wrs")
                    if gw >= 1:
                        cx.wait(s, "wrq", gw)
                    cx.inc(s.activation(db[:, :], wstrip[:, :], AF.Identity,
                                        scale=rs_w[:, 0:1]), "wt0")

                @blk.vector
                def _(v):
                    cx.wait(v, "wt0")
                    v.tensor_copy(i32b[:, :], db[:, :])
                    v.tensor_tensor(c1b[:, :], db[:, :], i32b[:, :], ALU.subtract)
                    v.scalar_tensor_tensor(db[:, :], c1b[:, :], -0.5, i32b[:, :],
                                           ALU.is_lt, ALU.subtract)
                    cx.inc(v.scalar_tensor_tensor(wq_nat[:, :], c1b[:, :], 0.5,
                                                  db[:, :], ALU.is_gt, ALU.subtract),
                           "wrq")

                @blk.tensor
                def _(t):
                    cx.wait(t, "wrq")
                    if gw == 0:
                        cx.wait(t, "idb")
                    cx.sem("wtev")
                    base_te = cx.cnt["wtev"]
                    for j in range(DT):
                        if j >= 2:
                            cx.wait(t, "wtev", base_te + j - 1)
                        cx.inc(t.transpose(ptb[j % 2][:, 0:128],
                                           wq_nat[:, j * 128:(j + 1) * 128],
                                           identb[:]), "wtr")

                @blk.vector
                def _(v):
                    for j in range(DT):
                        cx.wait(v, "wtr", cx.cnt["wtr"] - (DT - 1 - j))
                        cx.inc(v.tensor_copy(
                            wqt[:, j * DI + it * 128: j * DI + (it + 1) * 128],
                            ptb[j % 2][:, 0:128]), "wtev")

                @blk.tensor
                def _(t):
                    cx.wait(t, "wtev")

    with nc.Block() as blk:
        @blk.vector
        def _(v):
            cx.wait(v, "wsc")
            cx.wait(v, "rbx2")
            for it in range(IT):
                v.tensor_scalar_mul(sg[:, it:it + 1], sg[:, it:it + 1],
                                    sx_bc[:, 0:1])
                cx.inc(v.tensor_scalar_mul(su[:, it:it + 1], su[:, it:it + 1],
                                           sx_bc[:, 0:1]), "sev")
            cx.vsync(v)

    for c in range(NCH):
        with nc.Block() as blk:
            @blk.sync
            def _(sync):
                if c > 0:
                    cx.wait(sync, "chend4", c)
                cx.dma(sync.dma_start(
                    lstage.ap().rearrange("p (a d) -> p a d", d=D),
                    ln2_in.ap()[c * TC:(c + 1) * TC, :].rearrange(
                        "(a p) d -> p a d", p=128)), "ldx")

            @blk.scalar
            def _(s):
                cx.wait(s, "ldx")
                if c == 0:
                    cx.wait(s, "rbx2")
                for a in range(TT):
                    g = c * TT + a
                    if g >= 1:
                        cx.wait(s, "xqr", g)
                    cx.inc(s.activation(tq[:, :], lstage[:, a * D:(a + 1) * D],
                                        AF.Identity, scale=rsx_bc[:, 0:1]), "xqs")

            @blk.vector
            def _(v):
                for a in range(TT):
                    g = c * TT + a
                    cx.wait(v, "xqs", g + 1)
                    v.tensor_copy(i32b[:, :], tq[:, :])
                    v.tensor_tensor(db[:, :], tq[:, :], i32b[:, :], ALU.subtract)
                    v.scalar_tensor_tensor(c1b[:, :], db[:, :], -0.5, i32b[:, :],
                                           ALU.is_lt, ALU.subtract)
                    v.scalar_tensor_tensor(tq[:, :], db[:, :], 0.5, c1b[:, :],
                                           ALU.is_gt, ALU.subtract)
                    cx.inc(v.tensor_scalar(xq_tm[:, a * D:(a + 1) * D], tq[:, :],
                                           QMAX, -QMAX - 1.0, ALU.min, ALU.max),
                           "xqr")

            @blk.tensor
            def _(t):
                cx.sem("xqev")
                cx.wait(t, "xqr")
                for k in range(DT):
                    for a in range(TT):
                        g = k * TT + a
                        if cx.cnt["xqev"] + g >= 2:
                            cx.wait(t, "xqev", cx.cnt["xqev"] + g - 1)
                        cx.inc(t.transpose(
                            ptb[g % 2][:, 0:128],
                            xq_tm[:, a * D + k * 128: a * D + (k + 1) * 128],
                            identb[:]), "xqtr")

            @blk.vector
            def _(v):
                base = cx.cnt["xqtr"] - DT * TT
                for k in range(DT):
                    for a in range(TT):
                        g = k * TT + a
                        cx.wait(v, "xqtr", base + g + 1)
                        cx.inc(v.tensor_copy(
                            xqT[:, k * TC + a * 128: k * TC + (a + 1) * 128],
                            ptb[g % 2][:, 0:128]), "xqev")

        for it in range(IT):
            gi = c * IT + it
            with nc.Block() as blk:
                pg = pmm[2 * (it % 2)]
                pu = pmm[2 * (it % 2) + 1]

                @blk.tensor
                def _(t):
                    if it == 0:
                        cx.wait(t, "xqev")
                    if gi >= 2:
                        cx.wait(t, "hev", gi - 1)
                    for pi, (pm, wqt) in enumerate(((pg, wgt), (pu, wut))):
                        first = True
                        for k in range(DT):
                            lhs = wqt[:, k * DI + it * 128: k * DI + (it + 1) * 128]
                            rhs = xqT[:, k * TC:(k + 1) * TC]
                            mi = t.matmul(pm[:, 0:512], lhs, rhs, start=first,
                                          stop=(k == DT - 1))
                            if pi == 1 and k == DT - 1:
                                cx.inc(mi, "mg")
                            first = False

                @blk.scalar
                def _(s):
                    cx.wait(s, "mg")
                    if gi >= 1:
                        cx.wait(s, "hev", gi)
                    cx.inc(s.activation(silu[:, :], pg[:, 0:512], AF.Silu,
                                        scale=sg[:, it:it + 1]), "sil")

                @blk.vector
                def _(v):
                    cx.wait(v, "sil")
                    if gi >= 1:
                        cx.wait(v, "sth", 16 * gi)
                    v.scalar_tensor_tensor(hst[:, :], pu[:, 0:512], su[:, it:it + 1],
                                           silu[:, :], ALU.mult, ALU.mult)
                    v.reduce_max(amaxt[:], hst[:, :], axis=AX.X,
                                 apply_absolute_value=True)
                    cx.vsync(v)
                    v.tensor_max(amax[:], amax[:], amaxt[:])
                    cx.inc(v.tensor_copy(amaxt[:, 0:1], amaxt[:, 0:1]), "hev")

                @blk.sync
                def _(sync):
                    cx.wait(sync, "hev")
                    cx.dma(sync.dma_start(
                        h_out.ap()[it * 128:(it + 1) * 128, c * TC:(c + 1) * TC],
                        hst[:, :]), "sth")

        with nc.Block() as blk:
            @blk.sync
            def _(sync):
                cx.wait(sync, "sth")
                cx.inc(sync.nop(), "chend4")

    cx.cnt["allmax"] = cx.cnt["chend4"]
    cx.sems["allmax"] = cx.sems["chend4"]
    emit_localmax_finish(cx, nc, amax, ident, ptr, fin, fin2, lmax2_out)
    stack.close()
    return nc


def build_l5():
    nc = bass.Bass("TRN2", target_bir_lowering=False, debug=False,
                   detect_race_conditions=False)
    h_in = nc.dram_tensor("h", [DI, TPC], F32, kind="ExternalInput")
    lmax_in = nc.dram_tensor("lmaxs", [1, NCORE], F32, kind="ExternalInput")
    clip_in = nc.dram_tensor("clip_down", [1, 1], F32, kind="ExternalInput")
    wdn_in = nc.dram_tensor("wdn", [D, DI], F32, kind="ExternalInput")
    x2_in = nc.dram_tensor("x2", [TPC, D], F32, kind="ExternalInput")
    out_out = nc.dram_tensor("out", [TPC, D], F32, kind="ExternalOutput")

    stack = contextlib.ExitStack()
    cx = Ctx(nc, stack)
    TC = 256
    NCH = TPC // TC
    TT = TC // 128
    EB = 2

    ident = cx.sb("ident", [128, 128], F32)
    identb = cx.sb("identb", [128, 128], BF16)
    wstrip = cx.sb("wstrip", [128, DI], F32)
    wq_nat = cx.sb("wq_nat", [128, DI], BF16)
    wdt = cx.sb("wdt", [128, IT * D], BF16)
    swd_row = cx.sb("swd_row", [1, D], F32)
    swd_bc = cx.sb("swd_bc", [128, D], F32)
    rowm = cx.sb("rowm", [128, 1], F32)
    rs_w = cx.sb("rs_w", [128, 1], F32)
    sw_col = cx.sb("sw_col", [128, DT], F32)
    i32b = cx.sb("i32b", [128, DI], I32)
    db = cx.sb("db", [128, DI], F32)
    c1b = cx.sb("c1b", [128, DI], F32)
    gl = cx.sb("gl", [1, NCORE], F32)
    gmax = cx.sb("gmax", [1, 1], F32)
    clip_sb = cx.sb("clip_sb", [1, 1], F32)
    sh = cx.sb("sh", [1, 1], F32)
    rsh = cx.sb("rsh", [1, 1], F32)
    onesr = cx.sb("onesr", [1, 128], F32)
    rsh_bc = cx.sb("rsh_bc", [128, 1], F32)
    hstage = cx.sb("hstage", [128, IT * TC], F32)
    hq = cx.sb("hq", [128, IT * TC], BF16)
    tq = cx.sb("tq", [128, TC], F32)
    i32s = cx.sb("i32s", [128, TC], I32)
    dbs = cx.sb("dbs", [128, TC], F32)
    c1s = cx.sb("c1s", [128, TC], F32)
    x2stage = cx.sb("x2stage", [128, TT * D], F32)
    outst = cx.sb("outst", [128, TT * D], F32)

    ptr = [cx.ps(f"ptr{i}", [128, 128], F32) for i in range(2)]
    ptb = [cx.ps(f"ptb{i}", [128, 128], BF16) for i in range(2)]
    pmm = [cx.ps(f"pmm{i}", [128, 512], F32) for i in range(2)]

    with nc.Block() as blk:
        emit_ident(cx, blk, ident, identb)

        @blk.sync
        def _(sync):
            cx.dma(sync.dma_start(gl[0:1, :], lmax_in.ap()[0:1, :]), "ldc")
            cx.dma(sync.dma_start(clip_sb[0:1, :], clip_in.ap()[0:1, :]), "ldc")

        @blk.vector
        def _(v):
            cx.wait(v, "ldc")
            cx.inc(v.reduce_max(gmax[0:1, 0:1], gl[0:1, :], axis=AX.X), "gm")
            cx.vsync(v)
            v.memset(onesr[0:1, :], 1.0)

        @blk.scalar
        def _(s):
            cx.wait(s, "gm")
            cx.inc(s.activation(sh[0:1, :], clip_sb[0:1, :], AF.Sigmoid), "sg1")

        @blk.vector
        def _(v):
            cx.wait(v, "sg1")
            v.tensor_mul(sh[0:1, :], sh[0:1, :], gmax[0:1, :])
            cx.vsync(v)
            v.tensor_scalar_mul(sh[0:1, :], sh[0:1, :], 1.0 / QMAX)
            cx.vsync(v)
            cx.inc(v.reciprocal(rsh[0:1, :], sh[0:1, :]), "rsh1")
            cx.vsync(v)

        @blk.tensor
        def _(t):
            cx.wait(t, "rsh1")
            cx.inc(t.matmul(ptr[0][:, 0:1], onesr[0:1, :], rsh[0:1, 0:1],
                            start=True, stop=True), "rbx")

        @blk.vector
        def _(v):
            cx.wait(v, "rbx")
            cx.inc(v.tensor_copy(rsh_bc[:, 0:1], ptr[0][:, 0:1]), "rbx2")

        @blk.tensor
        def _(t):
            cx.wait(t, "rbx2")

    for jt in range(DT):
        with nc.Block() as blk:
            @blk.sync
            def _(sync):
                if jt > 0:
                    cx.wait(sync, "wt0", jt)
                cx.dma(sync.dma_start(wstrip[:, :],
                                      wdn_in.ap()[jt * 128:(jt + 1) * 128, :]), "ldw")

            @blk.vector
            def _(v):
                cx.wait(v, "ldw")
                v.reduce_max(rowm[:], wstrip[:, :], axis=AX.X,
                             apply_absolute_value=True)
                cx.vsync(v)
                v.reciprocal(rs_w[:], rowm[:])
                cx.vsync(v)
                cx.inc(v.tensor_scalar_mul(rs_w[:], rs_w[:], QMAX), "wrs")
                cx.inc(v.tensor_scalar_mul(sw_col[:, jt:jt + 1], rowm[:],
                                           1.0 / QMAX), "wsc")

            @blk.scalar
            def _(s):
                cx.wait(s, "wrs")
                if jt >= 1:
                    cx.wait(s, "wrq", jt)
                cx.inc(s.activation(db[:, :], wstrip[:, :], AF.Identity,
                                    scale=rs_w[:, 0:1]), "wt0")

            @blk.vector
            def _(v):
                cx.wait(v, "wt0")
                v.tensor_copy(i32b[:, :], db[:, :])
                v.tensor_tensor(c1b[:, :], db[:, :], i32b[:, :], ALU.subtract)
                v.scalar_tensor_tensor(db[:, :], c1b[:, :], -0.5, i32b[:, :],
                                       ALU.is_lt, ALU.subtract)
                cx.inc(v.scalar_tensor_tensor(wq_nat[:, :], c1b[:, :], 0.5, db[:, :],
                                              ALU.is_gt, ALU.subtract), "wrq")

            @blk.tensor
            def _(t):
                cx.wait(t, "wrq")
                if jt == 0:
                    cx.wait(t, "idb")
                cx.sem("wtev")
                base_te = cx.cnt["wtev"]
                for j in range(IT):
                    if j >= 2:
                        cx.wait(t, "wtev", base_te + j - 1)
                    cx.inc(t.transpose(ptb[j % 2][:, 0:128],
                                       wq_nat[:, j * 128:(j + 1) * 128],
                                       identb[:]), "wtr")

            @blk.vector
            def _(v):
                for j in range(IT):
                    cx.wait(v, "wtr", cx.cnt["wtr"] - (IT - 1 - j))
                    cx.inc(v.tensor_copy(
                        wdt[:, j * D + jt * 128: j * D + (jt + 1) * 128],
                        ptb[j % 2][:, 0:128]), "wtev")

            @blk.tensor
            def _(t):
                cx.wait(t, "wtev")

    with nc.Block() as blk:
        @blk.tensor
        def _(t):
            cx.wait(t, "wsc")
            cx.sem("srev")
            base_sr = cx.cnt["srev"]
            for jt in range(DT):
                if jt >= 2:
                    cx.wait(t, "srev", base_sr + jt - 1)
                cx.inc(t.transpose(ptr[jt % 2][0:1, 0:128], sw_col[:, jt:jt + 1],
                                   ident[:]), "str")

        @blk.vector
        def _(v):
            for jt in range(DT):
                cx.wait(v, "str", cx.cnt["str"] - (DT - 1 - jt))
                cx.inc(v.tensor_copy(swd_row[0:1, jt * 128:(jt + 1) * 128],
                                     ptr[jt % 2][0:1, 0:128]), "srev")
            cx.wait(v, "rsh1")
            cx.inc(v.tensor_scalar_mul(swd_row[0:1, :], swd_row[0:1, :],
                                       sh[0:1, 0:1]), "srow")

        @blk.tensor
        def _(t):
            cx.wait(t, "srow")
            for eb in range(EB):
                cx.inc(t.matmul(pmm[eb][:, 0:512], onesr[0:1, :],
                                swd_row[0:1, eb * 512:(eb + 1) * 512],
                                start=True, stop=True), "sbc")

        @blk.vector
        def _(v):
            cx.wait(v, "sbc")
            for eb in range(EB):
                cx.inc(v.tensor_copy(swd_bc[:, eb * 512:(eb + 1) * 512],
                                     pmm[eb][:, 0:512]), "sbv")

        @blk.tensor
        def _(t):
            cx.wait(t, "sbv")

    for c in range(NCH):
        with nc.Block() as blk:
            @blk.sync
            def _(sync):
                if c > 0:
                    cx.wait(sync, "chend5", c)
                cx.dma(sync.dma_start(
                    hstage.ap().rearrange("p (k t) -> p k t", t=TC),
                    h_in.ap()[:, c * TC:(c + 1) * TC].rearrange(
                        "(k p) t -> p k t", p=128)), "ldh")
                cx.dma(sync.dma_start(
                    x2stage.ap().rearrange("p (a d) -> p a d", d=D),
                    x2_in.ap()[c * TC:(c + 1) * TC, :].rearrange(
                        "(a p) d -> p a d", p=128)), "ldh")

            @blk.scalar
            def _(s):
                cx.wait(s, "ldh")
                if c == 0:
                    cx.wait(s, "rbx2")
                for k in range(IT):
                    g = c * IT + k
                    if g >= 1:
                        cx.wait(s, "hqr", g)
                    cx.inc(s.activation(tq[:, :], hstage[:, k * TC:(k + 1) * TC],
                                        AF.Identity, scale=rsh_bc[:, 0:1]), "hqs")

            @blk.vector
            def _(v):
                for k in range(IT):
                    g = c * IT + k
                    cx.wait(v, "hqs", g + 1)
                    v.tensor_copy(i32s[:, :], tq[:, :])
                    v.tensor_tensor(dbs[:, :], tq[:, :], i32s[:, :], ALU.subtract)
                    v.scalar_tensor_tensor(c1s[:, :], dbs[:, :], -0.5, i32s[:, :],
                                           ALU.is_lt, ALU.subtract)
                    v.scalar_tensor_tensor(tq[:, :], dbs[:, :], 0.5, c1s[:, :],
                                           ALU.is_gt, ALU.subtract)
                    cx.inc(v.tensor_scalar(hq[:, k * TC:(k + 1) * TC], tq[:, :],
                                           QMAX, -QMAX - 1.0, ALU.min, ALU.max),
                           "hqr")

        for a in range(TT):
            gt = c * TT + a
            with nc.Block() as blk:
                @blk.tensor
                def _(t):
                    if a == 0:
                        cx.wait(t, "hqr")
                    for eb in range(EB):
                        gg = 2 * gt + eb
                        if gg >= 2:
                            cx.wait(t, "oev", gg - 1)
                        first = True
                        for k in range(IT):
                            lhs = hq[:, k * TC + a * 128: k * TC + (a + 1) * 128]
                            rhs = wdt[:, k * D + eb * 512: k * D + (eb + 1) * 512]
                            mi = t.matmul(pmm[eb][:, 0:512], lhs, rhs, start=first,
                                          stop=(k == IT - 1))
                            if k == IT - 1:
                                cx.inc(mi, "md")
                            first = False

                @blk.vector
                def _(v):
                    x2row = x2stage[:, a * D:(a + 1) * D]
                    orow = outst[:, a * D:(a + 1) * D]
                    if gt >= TT:
                        cx.wait(v, "sto", 16 * c)
                    for eb in range(EB):
                        cx.wait(v, "md", cx.cnt["md"] - (EB - 1 - eb))
                        sl = slice(eb * 512, (eb + 1) * 512)
                        v.tensor_mul(orow[:, sl], pmm[eb][:, 0:512], swd_bc[:, sl])
                        cx.inc(v.tensor_add(orow[:, sl], orow[:, sl], x2row[:, sl]),
                               "oev")

        with nc.Block() as blk:
            @blk.sync
            def _(sync):
                cx.wait(sync, "oev")
                cx.dma(sync.dma_start(
                    out_out.ap()[c * TC:(c + 1) * TC, :].rearrange(
                        "(a p) d -> p a d", p=128),
                    outst.ap().rearrange("p (a d) -> p a d", d=D)), "sto")
                cx.wait(sync, "sto")
                cx.inc(sync.nop(), "chend5")

    with nc.Block() as blk:
        @blk.sync
        def _(sync):
            cx.wait(sync, "sto")
    stack.close()
    return nc


_CACHE = {}


def _get(name, builder):
    if name not in _CACHE:
        _CACHE[name] = builder()
    return _CACHE[name]


def kernel(L0_attn_x, L0_attn_Wd, L0_attn_bd, L0_attn_a_log, L0_attn_b_param,
           L0_attn_Wo, L0_attn_bo, L0_ffn_clip_x, L0_ffn_clip_down,
           L0_ffn_w_gate, L0_ffn_w_up, L0_ffn_w_down):
    x = np.ascontiguousarray(np.asarray(L0_attn_x, dtype=np.float32))
    wd = np.ascontiguousarray(np.asarray(L0_attn_Wd, dtype=np.float32))
    bd = np.ascontiguousarray(np.asarray(L0_attn_bd, dtype=np.float32))
    alog = np.ascontiguousarray(np.asarray(L0_attn_a_log, dtype=np.float32))
    bpar = np.ascontiguousarray(np.asarray(L0_attn_b_param, dtype=np.float32))
    wo = np.ascontiguousarray(np.asarray(L0_attn_Wo, dtype=np.float32))
    bo = np.ascontiguousarray(np.asarray(L0_attn_bo, dtype=np.float32))
    clip_x = np.asarray(L0_ffn_clip_x, dtype=np.float32).reshape(1, 1)
    clip_dn = np.asarray(L0_ffn_clip_down, dtype=np.float32).reshape(1, 1)
    wg = np.ascontiguousarray(np.asarray(L0_ffn_w_gate, dtype=np.float32))
    wu = np.ascontiguousarray(np.asarray(L0_ffn_w_up, dtype=np.float32))
    wdn = np.ascontiguousarray(np.asarray(L0_ffn_w_down, dtype=np.float32))

    cores = list(range(NCORE))
    xsh = [np.ascontiguousarray(x[b, q * TPC:(q + 1) * TPC, :])
           for b in range(B) for q in range(4)]

    nc1 = _get("l1", build_l1)
    r1 = run_bass_kernel_spmd(
        nc1, [{"x": xsh[i], "wd": wd, "bd": bd} for i in range(NCORE)],
        core_ids=cores)
    delta = [r["delta"] for r in r1.results]
    vv = [r["vv"] for r in r1.results]

    def dslice(parts, k):
        rows = slice(k * 128, (k + 1) * 128)
        return np.ascontiguousarray(np.concatenate(
            [parts[b * 4 + q][rows, :] for b in range(B) for q in range(4)], axis=1))

    nc2 = _get("l2", build_l2)
    in2 = [{"delta_sl": dslice(delta, k), "v_sl": dslice(vv, k),
            "alog_sl": np.ascontiguousarray(alog[k * 128:(k + 1) * 128, :]),
            "bpar_sl": np.ascontiguousarray(bpar[k * 128:(k + 1) * 128, :])}
           for k in range(NCORE)]
    r2 = run_bass_kernel_spmd(nc2, in2, core_ids=cores)
    ysl = [r["y_sl"] for r in r2.results]

    def ytok(i):
        b, q = divmod(i, 4)
        cols = slice(b * T + q * TPC, b * T + (q + 1) * TPC)
        return np.ascontiguousarray(
            np.concatenate([ysl[k][:, cols] for k in range(NCORE)], axis=0))

    nc3 = _get("l3", build_l3)
    r3 = run_bass_kernel_spmd(
        nc3, [{"x": xsh[i], "y": ytok(i), "wo": wo, "bo": bo} for i in range(NCORE)],
        core_ids=cores)
    x2 = [r["x2"] for r in r3.results]
    ln2 = [r["ln2"] for r in r3.results]
    lmax1 = np.ascontiguousarray(
        np.concatenate([r["lmax"].reshape(1, 1) for r in r3.results], axis=1))

    nc4 = _get("l4", build_l4)
    r4 = run_bass_kernel_spmd(
        nc4, [{"ln2": ln2[i], "lmaxs": lmax1, "clip_x": clip_x, "wg": wg, "wu": wu}
              for i in range(NCORE)], core_ids=cores)
    hsh = [r["h"] for r in r4.results]
    lmax2 = np.ascontiguousarray(
        np.concatenate([r["lmax2"].reshape(1, 1) for r in r4.results], axis=1))

    nc5 = _get("l5", build_l5)
    r5 = run_bass_kernel_spmd(
        nc5, [{"h": hsh[i], "lmaxs": lmax2, "clip_down": clip_dn, "wdn": wdn,
               "x2": x2[i]} for i in range(NCORE)], core_ids=cores)
    outs = [r["out"] for r in r5.results]

    out = np.empty((B, T, D), dtype=np.float32)
    for b in range(B):
        for q in range(4):
            out[b, q * TPC:(q + 1) * TPC, :] = outs[b * 4 + q]
    return out
